# version 03
# baseline (speedup 1.0000x reference)
import numpy as np
import ml_dtypes
import concourse.bass as bass
import concourse.bacc as bacc
import concourse.mybir as mybir
import bass_rust
from concourse.bass_utils import run_bass_kernel_spmd

F32 = mybir.dt.float32
BF16 = mybir.dt.bfloat16
AF = mybir.ActivationFunctionType
OP = mybir.AluOpType
AX = mybir.AxisListType
RED = bass_rust.ReduceOp

D, S, H = 1024, 256, 4096
NT, NOWN = 2050, 2048
NTILE, NTILES = 410, 5
SLAB, SLABH, NSLABS = 256, 258, 8
MAGIC = 12582912.0
CLIP149 = 1.49999988
DC, SC, HC = 8, 2, 32


import os as _os


class Eng:
    def __init__(self, nc, raw, name):
        self.raw, self.name = raw, name
        self.sem = nc.alloc_semaphore("sem_" + name)
        self.n = 0
        self.hw = {}
        self.serialize = (name in ("G", "V", "S")) and _os.environ.get("MK_SER", "1") == "1"

    def wait(self, *tokens, force_self=False):
        for t in tokens:
            if t is None:
                continue
            src, n = t
            if (src is self and not force_self) or self.hw.get(src, 0) >= n:
                continue
            self.raw.wait_ge(src.sem, n)
            self.hw[src] = n

    def op(self, method, *a, wait=(), inc=True, **kw):
        self.wait(*wait)
        if self.serialize and self.n > 0:
            self.raw.wait_ge(self.sem, self.n)
        inst = getattr(self.raw, method)(*a, **kw)
        if inc:
            inst.then_inc(self.sem, 1)
            self.n += 1
        return (self, self.n)


class Dma:
    def __init__(self, nc, eng, name):
        self.eng = eng
        self.sem = nc.alloc_semaphore("dsem_" + name)
        self.n = 0

    def dma(self, out, in_, wait=()):
        self.eng.wait(*wait, force_self=True)
        self.eng.raw.dma_start(out=out, in_=in_).then_inc(self.sem, 16)
        self.n += 16
        return (self, self.n)


def build_program(debug=False):
    nc = bacc.Bacc("TRN2", target_bir_lowering=False, debug=False)

    dx = nc.dram_tensor("x", [D, NT], F32, kind="ExternalInput").ap()
    dWin = nc.dram_tensor("W_in", [D, 3 * S], F32, kind="ExternalInput").ap()
    dwnorm = nc.dram_tensor("w_norm", [128, 8], F32, kind="ExternalInput").ap()
    dbinu = nc.dram_tensor("b_inu", [128, 4], F32, kind="ExternalInput").ap()
    dbing = nc.dram_tensor("b_ing", [128, 2], F32, kind="ExternalInput").ap()
    dWx = nc.dram_tensor("W_x", [128, 2 * S], F32, kind="ExternalInput").ap()
    dbx = nc.dram_tensor("b_x", [S], F32, kind="ExternalInput").ap()
    dWdt = nc.dram_tensor("W_dt", [128, 2 * S], F32, kind="ExternalInput").ap()
    dbdt = nc.dram_tensor("b_dt", [S], F32, kind="ExternalInput").ap()
    dWout = nc.dram_tensor("W_out", [128, 2 * D], F32, kind="ExternalInput").ap()
    dbout = nc.dram_tensor("b_out", [128, 8], F32, kind="ExternalInput").ap()
    dw1T = nc.dram_tensor("w1T", [D, H], F32, kind="ExternalInput").ap()
    dw2T = nc.dram_tensor("w2T", [H, D], F32, kind="ExternalInput").ap()
    dcw = nc.dram_tensor("conv_w", [128, 96], F32, kind="ExternalInput").ap()
    dcb = nc.dram_tensor("conv_b", [128, 32], F32, kind="ExternalInput").ap()
    dident = nc.dram_tensor("ident", [128, 128], F32, kind="ExternalInput").ap()
    donehot = nc.dram_tensor("onehot", [128, 8], F32, kind="ExternalInput").ap()
    dmasks = nc.dram_tensor("masks", [128, 2], F32, kind="ExternalInput").ap()
    dout = nc.dram_tensor("y", [NOWN, D], F32, kind="ExternalOutput").ap()
    dx1pp = nc.dram_tensor("x1pp", [D, NT], F32).ap()
    dcsp = nc.dram_tensor("dcsp", [S, NT], F32).ap()
    dPsp = nc.dram_tensor("dPsp", [S, NT], F32).ap()
    dtgsp = nc.dram_tensor("dtgsp", [S, NT], F32).ap()
    dm1sp = nc.dram_tensor("dm1sp", [128, NT], F32).ap()
    drpsp = nc.dram_tensor("drpsp", [128, NT], F32).ap()
    dccin = nc.dram_tensor("ccin", [128, 4], F32)
    dccout = nc.dram_tensor("ccout", [8, 128, 4], F32)
    dbg = {}
    if debug:
        dbg["x1pp"] = nc.dram_tensor("x1pp_echo", [D, NT], F32, kind="ExternalOutput").ap()
        dbg["hs"] = nc.dram_tensor("hs_echo", [S, NT], F32, kind="ExternalOutput").ap()
        dbg["u"] = nc.dram_tensor("u_echo", [S, NT], F32, kind="ExternalOutput").ap()
        dbg["z"] = nc.dram_tensor("z_echo", [S, NT], F32, kind="ExternalOutput").ap()
        dbg["dt"] = nc.dram_tensor("dt_echo", [S, NT], F32, kind="ExternalOutput").ap()
        dbg["sm"] = nc.dram_tensor("sm_echo", [128, 64], F32, kind="ExternalOutput").ap()
        dbg["m1"] = nc.dram_tensor("m1_echo", [128, NT], F32, kind="ExternalOutput").ap()
        dbg["tg"] = nc.dram_tensor("tg_echo", [S, NT], F32, kind="ExternalOutput").ap()
        dbg["m1sb"] = nc.dram_tensor("m1sb_echo", [128, 16], F32, kind="ExternalOutput").ap()
        dbg["xq1e"] = nc.dram_tensor("xq1_echo", [128, 8 * SLABH], BF16, kind="ExternalOutput").ap()
        dbg["h2e"] = nc.dram_tensor("h2_echo", [128, HC * SLAB], F32, kind="ExternalOutput").ap()
        dbg["xq2e"] = nc.dram_tensor("xq2_echo", [128, HC * SLAB], BF16, kind="ExternalOutput").ap()
        dbg["sc2e"] = nc.dram_tensor("sc2_echo", [128, 8 * NTILE], F32, kind="ExternalOutput").ap()
        dbg["outfme"] = nc.dram_tensor("outfm_echo", [128, 8 * SLAB], F32, kind="ExternalOutput").ap()
        dbg["he"] = nc.dram_tensor("h_echo", [128, 2 * SLABH], F32, kind="ExternalOutput").ap()
        dbg["wns"] = nc.dram_tensor("wns_echo", [128, 8 * 768], F32, kind="ExternalOutput").ap()
        dbg["alb"] = nc.dram_tensor("alb_echo", [128, 5 * NTILE], F32, kind="ExternalOutput").ap()
        dbg["xcbe"] = nc.dram_tensor("xcb_echo", [128, 8 * NTILE], F32, kind="ExternalOutput").ap()
        dbg["m1e"] = nc.dram_tensor("m1_early", [128, NT], F32, kind="ExternalOutput").ap()

    T = Eng(nc, nc.tensor, "T")
    V = Eng(nc, nc.vector, "V")
    Sc = Eng(nc, nc.scalar, "S")
    G = Eng(nc, nc.gpsimd, "G")
    SY = Eng(nc, nc.sync, "SY")
    import os as _os
    NSTREAMS = int(_os.environ.get("MK_STREAMS", "1"))
    if NSTREAMS:
        dP0 = Dma(nc, SY, "p0")
        dXT = Dma(nc, SY, "xt")
        dLC = Dma(nc, SY, "lc")
        dLP = Dma(nc, SY, "lP")
        dLTG = Dma(nc, SY, "ltg")
        dLM1 = Dma(nc, SY, "lm1")
        dLRP = Dma(nc, SY, "lrp")
        dX1S = Dma(nc, SY, "x1s")
        dSPM1 = Dma(nc, Sc, "spm1")
        dSPRP = Dma(nc, Sc, "sprp")
        dSPTG = Dma(nc, Sc, "sptg")
        dSPCP = Dma(nc, Sc, "spcp")
        dX1PP = Dma(nc, Sc, "x1pp")
        dOUT0 = Dma(nc, Sc, "out0")
        dOUT1 = Dma(nc, Sc, "out1")
        dECHO = Dma(nc, Sc, "echo")
        dP1 = Dma(nc, G, "p1")
        dP2A = Dma(nc, G, "p2a")
        dP2B = Dma(nc, G, "p2b")
        dCC = Dma(nc, G, "cc")
    else:
        dSY_ = Dma(nc, SY, "sy")
        dSC_ = Dma(nc, Sc, "sc")
        dG_ = Dma(nc, G, "g")
        dP0 = dXT = dLC = dLP = dLTG = dLM1 = dLRP = dX1S = dSY_
        dSPM1 = dSPRP = dSPTG = dSPCP = dX1PP = dOUT0 = dOUT1 = dECHO = dSC_
        dP1 = dP2A = dP2B = dCC = dG_
    ccsem = nc.alloc_semaphore("ccsem")

    BASE, TOP = int(nc.sbuf_base), int(nc.sbuf_top)
    cur = [(BASE + 31) // 32 * 32]
    offsets = {}

    def carve(name, shape, dtype, at=None):
        per = int(np.prod(shape[1:])) * mybir.dt.size(dtype)
        off = at if at is not None else (cur[0] + 31) // 32 * 32
        assert off % 32 == 0, name
        if at is None:
            cur[0] = off + per
        assert off + per <= TOP, f"SBUF overflow: {name} ends at {off + per - BASE}"
        t = nc.alloc_sbuf_tensor_at(name, list(shape), dtype, offset=off)
        offsets[name] = (off, off + per)
        return t

    ident = carve("identt", [128, 128], F32)
    ones = carve("ones", [128, 128], F32)
    k127 = carve("k127", [128, SLABH], F32)
    bconst = carve("bconst", [128, 4], F32)
    bing = carve("bing", [128, 2], F32)
    bout_pp = carve("boutpp", [128, 8], F32)
    cw = carve("cw", [128, 96], F32)
    cb = carve("cb", [128, 32], F32)
    masks = carve("masksS", [128, 2], F32)
    onehot = carve("onehotS", [128, 8], F32)
    mcw = carve("mcw", [128, 2], F32)
    swpp = carve("swpp", [128, 2], F32)
    rw2 = carve("rw2", [128, 1], F32)
    onesbf = carve("onesbf", [1, 128], BF16)
    binu = carve("binu", [128, 4], F32)
    stg4 = carve("stg4", [128, 4], F32)
    gath = carve("gath", [128, 32], F32)
    Gpp = carve("Gpp", [128, 2], F32)
    sttmp = carve("sttmp", [128, 8], F32)
    wacc = carve("wacc", [128, 2], F32)
    absscr = carve("absscr", [128, 1], F32)
    carry = carve("carry", [128, 4], F32)
    wnchunks = carve("wnch", [128, 8], F32)
    m1row = carve("m1row", [1, NT], BF16)
    madj = carve("madj", [1, SLAB], BF16)
    ch = [carve(f"ch{i}", [128, NTILE], F32) for i in range(6)]
    s8 = carve("s8", [128, NTILE], F32)
    q8 = carve("q8", [128, NTILE], F32)
    sqscr = carve("sqscr", [128, NTILE], F32)
    sq2 = carve("sq2", [128, NTILE], F32)
    hscr = carve("hscr", [128, 2 * SLABH], F32)
    stage128 = carve("stage128", [128, 256], F32)

    w1qT = carve("w1qT", [128, 8 * H], BF16)
    w1o = offsets['w1qT'][0]
    arF = carve("arF", [128, 10 * NTILE], F32, at=w1o)
    Wns = carve("Wns", [128, 8 * 768], F32, at=w1o + 16416)
    wscr1 = carve("wscr1", [128, 2048], F32, at=w1o + 41024)
    Wxs = carve("Wxs", [128, 2 * S], F32, at=w1o + 49248)
    Wdts = carve("Wdts", [128, 2 * S], F32, at=w1o + 51296)
    bxr = carve("bxr", [1, 2 * S], F32, at=w1o + 53344)
    brow = carve("brow", [1, 2 * S], F32, at=w1o + 55392)
    zeros410 = carve("zeros410", [128, NTILE], F32, at=w1o + 57440)
    tgstage = carve("tgstage", [128, 2 * NTILE], F32, at=w1o + 59104)
    assert w1o + 59104 + 3280 <= offsets['w1qT'][1]

    w2qT = carve("w2qT", [128, 32 * D], BF16)
    w2o = offsets['w2qT'][0]
    xt = carve("xt", [128, 8 * NTILE], F32, at=w2o)
    xcb = carve("xcb", [128, 8 * NTILE], F32, at=w2o + 13120)
    Wouts = carve("Wouts", [128, 2 * D], F32, at=w2o + 26240)
    cpstage = carve("cpstage", [128, 4 * NTILE], F32, at=w2o + 34432)
    ctile = carve("ctile", [128, 2 * NTILE], F32, at=w2o + 40992)
    Ptile = carve("Ptile", [128, 2 * NTILE], F32, at=w2o + 44288)
    tgtile = carve("tgtile", [128, 2 * NTILE], F32, at=w2o + 47584)
    m1tile = carve("m1tile", [128, NTILE], F32, at=w2o + 50880)
    rptile = carve("rptile", [128, NTILE], F32, at=w2o + 52544)
    assert w2o + 52544 + 1640 <= offsets['w2qT'][1]

    cbuf = carve("cbuf", [128, HC * SLAB], F32)
    cbo = offsets['cbuf'][0]
    xq2 = carve("xq2", [128, HC * SLAB], BF16, at=cbo)
    wscr2 = [carve("wscr2a", [128, D], F32, at=cbo),
             carve("wscr2b", [128, D], F32, at=cbo + 4096)]

    x1s = carve("x1s", [128, 8 * SLABH], F32)
    ystage = carve("ystage", [128, 2 * NTILE], F32, at=offsets['x1s'][0])
    tcent = carve("tcent", [128, 8 * SLABH], F32)
    xq1 = carve("xq1", [128, 8 * SLABH], BF16, at=offsets['tcent'][0])
    outfm = carve("outfm", [128, SLAB], F32)

    assert cur[0] <= TOP

    PS = nc.alloc_psum_tensor("PS", [128, 4096], F32)

    def bank(b, w):
        return PS[:, 512 * b: 512 * b + w]

    bank_cons = {b: None for b in range(8)}

    t_ident = dP0.dma(ident[:, :], dident)
    t_onehot = dP0.dma(onehot[:, :], donehot)
    t_masks = dP0.dma(masks[:, :], dmasks)
    t_wx = dP0.dma(Wxs[:, :], dWx)
    t_wdt = dP0.dma(Wdts[:, :], dWdt)
    t_wout = dP0.dma(Wouts[:, :], dWout)
    t_bout = dP0.dma(bout_pp[:, :], dbout)
    t_cw = dP0.dma(cw[:, :], dcw)
    t_cb = dP0.dma(cb[:, :], dcb)
    t_binu = dP0.dma(binu[:, :], dbinu)
    t_bing = dP0.dma(bing[:, :], dbing)
    t_bx = dP0.dma(bxr[0:1, 0:S], dbx[None, :])
    t_bdt = dP0.dma(bxr[0:1, S:2 * S], dbdt[None, :])
    t_wnorm = dP0.dma(wnchunks[:, :], dwnorm)
    t_wnld = [dP0.dma(Wns[:, c * 768:(c + 1) * 768], dWin[c * 128:(c + 1) * 128, :])
              for c in range(DC)]
    t_p0_all = t_wnld[-1]
    wn_done = [V.op("tensor_scalar_mul", Wns[:, c * 768:(c + 1) * 768],
                    Wns[:, c * 768:(c + 1) * 768], wnchunks[:, c:c + 1],
                    wait=(t_p0_all,)) for c in range(DC)]
    t_memz = G.op("memset", zeros410[:, :], 0.0)
    t_ones = G.op("memset", ones[:, :], 1.0)
    t_onesbf = G.op("memset", onesbf[:, :], 1.0)
    t_k127 = G.op("memset", k127[:, :], 127.0)
    t_wacc0 = V.op("memset", wacc[:, :], 0.0)

    t_bcmm = []
    for j, Wt in enumerate((Wxs, Wdts)):
        tk = None
        for k in range(SC):
            tk = T.op("matmul", PS[0:1, (2 + j) * 512:(2 + j) * 512 + S],
                      binu[:, 2 * j + k: 2 * j + k + 1], Wt[:, k * S:(k + 1) * S],
                      start=(k == 0), stop=(k == SC - 1),
                      wait=(t_p0_all, t_p0_all, t_p0_all), inc=(k == SC - 1))
        t_bcmm.append(tk)
    t_br0 = V.op("tensor_add", brow[0:1, 0:S], PS[0:1, 1024:1024 + S], bxr[0:1, 0:S],
                 wait=(t_bcmm[0], t_p0_all))
    t_br1 = V.op("tensor_add", brow[0:1, S:2 * S], PS[0:1, 1536:1536 + S], bxr[0:1, S:2 * S],
                 wait=(t_bcmm[1], t_p0_all))
    t_bc = []
    for j in range(4):
        ttr = T.op("transpose", PS[:, 2048 + 128 * j:2048 + 128 * j + 1],
                   brow[0:1, 128 * j:128 * (j + 1)], ident[0:1, 0:1],
                   wait=(t_br0, t_br1, t_p0_all))
        t_bc.append(V.op("tensor_copy", bconst[:, j:j + 1],
                         PS[:, 2048 + 128 * j:2048 + 128 * j + 1], wait=(ttr,)))
    bank_cons[2] = t_br0
    bank_cons[3] = t_br1
    bank_cons[4] = t_bc[3]
    bank_cons[5] = t_bc[3]

    prep1 = []
    p1_war = [t_wacc0]
    p1_add = [t_wacc0]

    def mk_p1(widx, src_ap, w):
        def job():
            tl = dP1.dma(wscr1[:, 0:w], src_ap, wait=(p1_war[0],))
            ta = Sc.op("activation", wscr1[:, 0:w], wscr1[:, 0:w], AF.Abs,
                       accum_out=absscr[:, :], wait=(tl, p1_add[0]))
            p1_war[0] = ta
            p1_add[0] = V.op("tensor_add", wacc[:, widx:widx + 1],
                             wacc[:, widx:widx + 1], absscr[:, :], wait=(ta,))
        return job

    for ck in range(DC):
        for j in range(2):
            prep1.append(mk_p1(0, dw1T[ck * 128:(ck + 1) * 128, j * 2048:(j + 1) * 2048], 2048))
    for ck in range(HC):
        prep1.append(mk_p1(1, dw2T[ck * 128:(ck + 1) * 128, :], 1024))

    def pump(jobs, n=None):
        n = len(jobs) if n is None else min(n, len(jobs))
        for _ in range(n):
            jobs.pop(0)()

    tu = arF[:, 0:2 * NTILE]
    tz = arF[:, 2 * NTILE:4 * NTILE]
    dtb = arF[:, 4 * NTILE:6 * NTILE]
    abuf = arF[:, 6 * NTILE:8 * NTILE]
    utb = arF[:, 8 * NTILE:10 * NTILE]
    scanC_prev = [None, None]
    scanP_prev = [None, None]
    t_cp_dma = t_tg_dma = t_m1_dma = t_rp_dma = None
    t_carry = t_sum_tok = None
    t_m1row = None
    t_echo = []
    cons_xt = None
    cons_xcb = None
    cons_s8 = None
    cons_chain = None
    t_stat_last = None
    t_T_loop1_last = None

    def xdma(n0, wait=()):
        src = dx.rearrange("(c p) n -> p c n", p=128)[:, :, n0:n0 + NTILE]
        return dXT.dma(xt[:, :].rearrange("p (c n) -> p c n", c=DC), src, wait=wait)

    t_xload = xdma(0)
    for n in range(NTILES):
        n0 = n * NTILE
        xch = [xt[:, c * NTILE:(c + 1) * NTILE] for c in range(DC)]
        tload = t_xload
        t_s = G.op("tensor_add", s8[:, :], xch[0], xch[1], wait=(tload, cons_s8))
        for c in range(2, DC):
            t_s = G.op("tensor_add", s8[:, :], s8[:, :], xch[c])
        t_q = None
        t_sqp = [t_stat_last, t_stat_last]
        for c in range(DC):
            scr = sqscr if c % 2 == 0 else sq2
            t_sq = Sc.op("activation", scr[:, :], xch[c], AF.Square,
                         wait=(tload, t_sqp[c % 2]))
            t_q = (V.op("tensor_copy", q8[:, :], scr[:, :], wait=(t_sq, cons_s8)) if c == 0
                   else V.op("tensor_add", q8[:, :], q8[:, :], scr[:, :], wait=(t_sq,)))
            t_sqp[c % 2] = t_q
        t_Sb = T.op("matmul", bank(6, NTILE), ones[:, :], s8[:, :], start=True, stop=True,
                    wait=(t_s, t_ones, bank_cons[6]))
        t_Qb = T.op("matmul", bank(7, NTILE), ones[:, :], q8[:, :], start=True, stop=True,
                    wait=(t_q, bank_cons[7]))
        cons_s8 = t_Qb
        m1b, alb, rpb, c0, c1, c2 = ch
        t_m1 = V.op("tensor_scalar_mul", m1b[:, :], bank(6, NTILE), 1.0 / D,
                    wait=(t_Sb, cons_chain, t_m1_dma, t_rp_dma, t_m1row))
        t_qq = V.op("tensor_scalar_mul", c0[:, :], bank(7, NTILE), 1.0 / D, wait=(t_Qb,))
        bank_cons[6] = t_m1
        bank_cons[7] = t_qq
        t_ms = Sc.op("activation", c1[:, :], m1b[:, :], AF.Square, wait=(t_m1,))
        t_var = V.op("tensor_sub", c2[:, :], c0[:, :], c1[:, :], wait=(t_ms, t_qq))
        t_vpe = V.op("tensor_scalar_add", c2[:, :], c2[:, :], 1e-5)
        t_rec = V.op("reciprocal", c0[:, :], c2[:, :], wait=(t_vpe,))
        t_r0 = Sc.op("activation", c1[:, :], c0[:, :], AF.Sqrt, wait=(t_rec,))
        t_n = V.op("tensor_mul", rpb[:, :], c1[:, :], c1[:, :], wait=(t_r0,))
        t_n = V.op("tensor_mul", rpb[:, :], rpb[:, :], c2[:, :])
        t_n = V.op("tensor_scalar", rpb[:, :], rpb[:, :], -0.5, 1.5, OP.mult, OP.add)
        t_r1 = V.op("tensor_mul", c1[:, :], c1[:, :], rpb[:, :])
        t_w8 = V.op("tensor_scalar", c2[:, :], c0[:, :], -1e-5, 1.0, OP.mult, OP.add)
        t_w8 = V.op("tensor_scalar_add", c2[:, :], c2[:, :], 1e-8)
        t_rc2 = V.op("reciprocal", c0[:, :], c2[:, :], wait=(t_w8,))
        t_rm = Sc.op("activation", alb[:, :], c0[:, :], AF.Sqrt,
                     wait=(t_rc2, t_sqp[0]) + tuple(t_echo))
        t_n2 = V.op("tensor_mul", sqscr[:, :], alb[:, :], alb[:, :], wait=(t_rm,))
        t_n2 = V.op("tensor_mul", sqscr[:, :], sqscr[:, :], c2[:, :])
        t_n2 = V.op("tensor_scalar", sqscr[:, :], sqscr[:, :], -0.5, 1.5, OP.mult, OP.add)
        t_rm1 = V.op("tensor_mul", alb[:, :], alb[:, :], sqscr[:, :])
        t_al = V.op("tensor_mul", alb[:, :], c1[:, :], alb[:, :], wait=(t_r1, t_rm1))
        t_rp = V.op("tensor_scalar_add", rpb[:, :], c1[:, :], 1.0, wait=(t_r1,))
        t_stat_last = t_rm1
        if debug:
            t_albe = dECHO.dma(dbg["alb"][:, n * NTILE:(n + 1) * NTILE], alb[:, :],
                               wait=(t_al, t_echo[-1] if t_echo else None))
            t_echo = [t_albe]
        t_m1_dma = dSPM1.dma(dm1sp[:, n0:n0 + NTILE], m1b[:, :], wait=(t_m1, t_m1_dma))
        if debug:
            t_m1_dma = dSPM1.dma(dbg["m1sb"][:, 3 * n:3 * n + 3], m1b[:, 0:3],
                               wait=(t_m1_dma,))
        t_rp_dma = dSPRP.dma(drpsp[:, n0:n0 + NTILE], rpb[:, :], wait=(t_rp, t_rp_dma))
        t_m1row = Sc.op("copy", out=m1row[0:1, n0:n0 + NTILE], in_=m1b[0:1, :], wait=(t_m1,))
        t_xc = [V.op("tensor_sub", xcb[:, c * NTILE:(c + 1) * NTILE], xch[c], m1b[:, :],
                     wait=(t_m1, tload) + ((cons_xcb,) if c == 0 else ()))
                for c in range(DC)]
        cons_xt = (t_xc[-1], t_s, t_q)
        cons_chain = (V, max(t_xc[-1][1], t_al[1]))
        t_U = []
        for m in range(6):
            tmm = None
            for k in range(DC):
                w0 = (t_xc[k], wn_done[k]) + ((bank_cons[m],) if k == 0 else ())
                tmm = T.op("matmul", bank(m, NTILE),
                           Wns[:, k * 768 + m * 128:k * 768 + m * 128 + 128],
                           xcb[:, k * NTILE:(k + 1) * NTILE],
                           start=(k == 0), stop=(k == DC - 1), wait=w0, inc=(k == DC - 1))
            t_U.append(tmm)
        cons_xcb = t_U[-1]
        if n + 1 < NTILES:
            t_xload = xdma((n + 1) * NTILE, wait=(t_xc[-1], t_s, t_q))
        t_epi = []
        for m in range(6):
            if m < 2:
                dst = tu[:, m * NTILE:(m + 1) * NTILE]
            elif m < 4:
                dst = tz[:, (m - 2) * NTILE:(m - 1) * NTILE]
            else:
                dst = tgstage[:, (m - 4) * NTILE:(m - 3) * NTILE]
            t_epi.append(V.op("tensor_mul", dst, bank(m, NTILE), alb[:, :],
                              wait=(t_U[m], t_al) + ((t_tg_dma,) if m >= 4 else ())
                                   + tuple(t_echo)))
            bank_cons[m] = t_epi[m]
        t_tg_dma = dSPTG.dma(dtgsp.rearrange("(g p) n -> p g n", p=128)[:, :, n0:n0 + NTILE],
                           tgstage[:, :].rearrange("p (g n) -> p g n", g=2),
                           wait=(t_epi[4], t_epi[5]))
        t_utmm, t_dtmm = [], []
        for m in range(SC):
            tmm = None
            for k in range(SC):
                w0 = (t_epi[0], t_epi[1]) + ((bank_cons[6 + m],) if k == 0 else ())
                tmm = T.op("matmul", bank(6 + m, NTILE),
                           Wxs[:, k * S + m * 128:k * S + m * 128 + 128],
                           tu[:, k * NTILE:(k + 1) * NTILE],
                           start=(k == 0), stop=(k == SC - 1), wait=w0, inc=(k == SC - 1))
            t_utmm.append(tmm)
        for m in range(SC):
            tmm = None
            for k in range(SC):
                w0 = (t_epi[2], t_epi[3]) + ((bank_cons[m],) if k == 0 else ())
                tmm = T.op("matmul", bank(m, NTILE),
                           Wdts[:, k * S + m * 128:k * S + m * 128 + 128],
                           tz[:, k * NTILE:(k + 1) * NTILE],
                           start=(k == 0), stop=(k == SC - 1), wait=w0, inc=(k == SC - 1))
            t_dtmm.append(tmm)
        t_T_loop1_last = t_dtmm[-1]
        t_ut, t_dt = [], []
        for m in range(SC):
            t_ut.append(Sc.op("activation", utb[:, m * NTILE:(m + 1) * NTILE],
                              bank(6 + m, NTILE), AF.Tanh, bias=bconst[:, m:m + 1],
                              scale=1.0, wait=(t_utmm[m], t_bc[m])))
            bank_cons[6 + m] = t_ut[m]
        for m in range(SC):
            t_dt.append(Sc.op("activation", dtb[:, m * NTILE:(m + 1) * NTILE],
                              bank(m, NTILE), AF.Sigmoid, bias=bconst[:, 2 + m:3 + m],
                              scale=1.0, wait=(t_dtmm[m], t_bc[2 + m]) + tuple(t_echo)))
            bank_cons[m] = t_dt[m]
        t_a = [V.op("tensor_scalar", abuf[:, m * NTILE:(m + 1) * NTILE],
                    dtb[:, m * NTILE:(m + 1) * NTILE], -1.0, 1.0, OP.mult, OP.add,
                    wait=(t_dt[m],)) for m in range(SC)]
        t_b = [V.op("tensor_mul", utb[:, m * NTILE:(m + 1) * NTILE],
                    dtb[:, m * NTILE:(m + 1) * NTILE], utb[:, m * NTILE:(m + 1) * NTILE],
                    wait=(t_dt[m], t_ut[m])) for m in range(SC)]
        if n == 0:
            t_a = [V.op("memset", abuf[:, m * NTILE:m * NTILE + 1], 1.0, wait=(t_a[m],))
                   for m in range(SC)]
            t_b = [V.op("memset", utb[:, m * NTILE:m * NTILE + 1], 0.0, wait=(t_b[m],))
                   for m in range(SC)]
        t_scan = []
        for m in range(SC):
            cini = 0.0 if n == 0 else carry[:, m:m + 1]
            pini = 1.0 if n == 0 else carry[:, 2 + m:3 + m]
            scanC_prev[m] = V.op("tensor_tensor_scan", cpstage[:, m * NTILE:(m + 1) * NTILE],
                                 abuf[:, m * NTILE:(m + 1) * NTILE],
                                 utb[:, m * NTILE:(m + 1) * NTILE],
                                 cini, OP.mult, OP.add,
                                 wait=(t_a[m], t_b[m], scanC_prev[m], t_carry, t_cp_dma))
            scanP_prev[m] = V.op("tensor_tensor_scan",
                                 cpstage[:, (2 + m) * NTILE:(3 + m) * NTILE],
                                 abuf[:, m * NTILE:(m + 1) * NTILE], zeros410[:, :],
                                 pini, OP.mult, OP.add,
                                 wait=(t_memz, scanP_prev[m], t_carry, t_cp_dma))
            t_scan += [scanC_prev[m], scanP_prev[m]]
        if debug:
            prev_echo = t_echo[-1] if t_echo else None
            t_echo = []
            for m in range(SC):
                prev_echo = dECHO.dma(dbg["u"][m * 128:(m + 1) * 128, n0:n0 + NTILE],
                        tu[:, m * NTILE:(m + 1) * NTILE],
                        wait=(t_epi[m], t_utmm[-1], prev_echo))
                prev_echo = dECHO.dma(dbg["z"][m * 128:(m + 1) * 128, n0:n0 + NTILE],
                        tz[:, m * NTILE:(m + 1) * NTILE],
                        wait=(t_epi[2 + m], t_dtmm[-1], prev_echo))
                prev_echo = dECHO.dma(dbg["dt"][m * 128:(m + 1) * 128, n0:n0 + NTILE],
                        dtb[:, m * NTILE:(m + 1) * NTILE],
                        wait=(t_dt[m], t_a[-1], t_b[-1], prev_echo))
            t_echo = [prev_echo]
        t_carry = V.op("tensor_copy", carry[:, :],
                       cpstage[:, :].rearrange("p (j n) -> p n j", j=4)
                       [:, NTILE - 1:NTILE, :].rearrange("p n j -> p (n j)"),
                       wait=tuple(t_scan))
        if n == NTILES - 1:
            t_sum_tok = V.op("tensor_copy", stg4[:, :],
                             cpstage[:, :].rearrange("p (j n) -> p n j", j=4)
                             [:, 408:409, :].rearrange("p n j -> p (n j)"),
                             wait=tuple(t_scan))
        t_cp_dma = dSPCP.dma(dcsp.rearrange("(g p) n -> p g n", p=128)[:, :, n0:n0 + NTILE],
                           cpstage[:, 0:2 * NTILE].rearrange("p (g n) -> p g n", g=2),
                           wait=tuple(t_scan))
        t_cp_dma = dSPCP.dma(dPsp.rearrange("(g p) n -> p g n", p=128)[:, :, n0:n0 + NTILE],
                           cpstage[:, 2 * NTILE:4 * NTILE].rearrange("p (g n) -> p g n", g=2),
                           wait=(t_cp_dma,))
        pump(prep1, 9)

    pump(prep1)
    if debug:
        Sc.wait(t_m1_dma)
        t_m1e = dECHO.dma(dbg["m1e"], dm1sp, wait=(t_m1_dma, t_echo[-1] if t_echo else None))
        t_wnse = dECHO.dma(dbg["wns"], Wns[:, :], wait=(t_T_loop1_last, t_m1e))
        t_xcbe = dECHO.dma(dbg["xcbe"], xcb[:, :], wait=(cons_xcb, t_wnse))
        t_echo = [t_xcbe]
        ex_w1_dbg = (t_wnse, t_xcbe)

    t_ccw = dCC.dma(dccin.ap()[:, 0:2], stg4[:, 2:4], wait=(t_sum_tok,))
    t_ccw = dCC.dma(dccin.ap()[:, 2:4], stg4[:, 0:2], wait=(t_ccw,))
    G.wait(t_ccw)
    G.raw.collective_compute("AllGather", OP.bypass, replica_groups=[list(range(8))],
                             ins=[dccin.ap().opt()], outs=[dccout.ap().opt()]) \
        .then_inc(ccsem, 1)
    G.raw.wait_ge(ccsem, 1)
    t_ccr = dCC.dma(gath[:, :].rearrange("p (j a) -> p j a", a=4),
                   dccout.ap().rearrange("j p a -> p j a"))
    t_G = []
    for m in range(SC):
        gath3 = gath[:, :].rearrange("p (j a) -> p j a", a=4)
        t1 = V.op("tensor_tensor_scan", sttmp[:, 0:8], gath3[:, :, m],
                  gath3[:, :, 2 + m], 0.0, OP.mult, OP.add, wait=(t_ccr,))
        t2 = V.op("tensor_mul", sttmp[:, 0:8], sttmp[:, 0:8], onehot[:, :],
                  wait=(t1, t_p0_all))
        t_G.append(V.op("tensor_reduce", Gpp[:, m:m + 1], sttmp[:, 0:8], AX.X, OP.add,
                        wait=(t2,)))

    t_mc = []
    for w in range(2):
        t_f = T.op("matmul", PS[0:1, 3072 + 512 * w:3072 + 512 * w + 1],
                   wacc[:, w:w + 1], ones[:, 0:1], start=True, stop=True,
                   wait=(p1_add[0], bank_cons[6], bank_cons[7]))
        t_v = V.op("tensor_scalar", sttmp[0:1, 6 + w:7 + w],
                   PS[0:1, 3072 + 512 * w:3072 + 512 * w + 1],
                   1.0 / float(D * H), 1e-5, OP.mult, OP.max, wait=(t_f,))
        t_b2 = T.op("matmul", PS[:, 3200 + 512 * w:3200 + 512 * w + 1],
                    ones[0:1, :], sttmp[0:1, 6 + w:7 + w], start=True, stop=True,
                    wait=(t_v,))
        t_mc.append(V.op("tensor_copy", mcw[:, w:w + 1],
                         PS[:, 3200 + 512 * w:3200 + 512 * w + 1], wait=(t_b2,)))
    bank_cons[6] = t_mc[1]
    bank_cons[7] = t_mc[1]
    t_rw2 = V.op("reciprocal", rw2[:, :], mcw[:, 1:2], wait=(t_mc[1],))
    t_sw = V.op("reciprocal", swpp[:, :], mcw[:, :], wait=(t_mc[0], t_mc[1]))

    p2_done = []
    p2_war = [t_sw, t_sw]

    p2_idx = [0]

    def mk_p2(widx, src_ap, dst_ap, extra=()):
        def job():
            i = p2_idx[0] % 2
            p2_idx[0] += 1
            sb = wscr2[i]
            w = 1024
            tl = (dP2A if i == 0 else dP2B).dma(sb[:, 0:w], src_ap, wait=(p2_war[i], t_sw) + tuple(extra))
            t1 = V.op("tensor_scalar", sb[:, 0:w], sb[:, 0:w],
                      swpp[:, widx:widx + 1], CLIP149, OP.mult, OP.min, wait=(tl,))
            t2 = V.op("tensor_scalar", sb[:, 0:w], sb[:, 0:w],
                      -CLIP149, MAGIC, OP.max, OP.add, wait=(t1,))
            t3 = Sc.op("activation", dst_ap, sb[:, 0:w],
                       AF.Copy, bias=-MAGIC, scale=1.0, wait=(t2,))
            p2_war[i] = t3
            p2_done.append(t3)
        return job

    prep2 = []
    ex_w1 = (t_sum_tok, p1_war[0], t_T_loop1_last, t_tg_dma)
    if debug:
        ex_w1 = ex_w1 + ex_w1_dbg
    for ck in range(DC):
        for pc in range(4):
            prep2.append(mk_p2(0, dw1T[ck * 128:(ck + 1) * 128, pc * 1024:(pc + 1) * 1024],
                               w1qT[:, ck * H + pc * 1024: ck * H + (pc + 1) * 1024],
                               extra=ex_w1 if ck == 0 else ()))
    pump(prep2)
    t_w1q_done = [p2_done[-1], p2_done[-2]]

    t_x1dma = None
    t_hs_echo = t_echo[-1] if (debug and t_echo) else None
    lp2_cons = {"c": None, "P": None, "tg": None, "m1": None, "rp": None}
    cons_ystage = None
    t_xload = xdma(0, wait=cons_xt)
    for n in range(NTILES):
        n0 = n * NTILE
        t_lc = dLC.dma(ctile[:, :].rearrange("p (g n) -> p g n", g=2),
                       dcsp.rearrange("(g p) n -> p g n", p=128)[:, :, n0:n0 + NTILE],
                       wait=(t_cp_dma, lp2_cons["c"]))
        t_lP = dLP.dma(Ptile[:, :].rearrange("p (g n) -> p g n", g=2),
                       dPsp.rearrange("(g p) n -> p g n", p=128)[:, :, n0:n0 + NTILE],
                       wait=(t_cp_dma, lp2_cons["P"]))
        t_ltg = dLTG.dma(tgtile[:, :].rearrange("p (g n) -> p g n", g=2),
                        dtgsp.rearrange("(g p) n -> p g n", p=128)[:, :, n0:n0 + NTILE],
                        wait=(t_tg_dma, lp2_cons["tg"]))
        t_lm1 = dLM1.dma(m1tile[:, :], dm1sp[:, n0:n0 + NTILE],
                        wait=(t_m1_dma, lp2_cons["m1"]))
        t_lrp = dLRP.dma(rptile[:, :], drpsp[:, n0:n0 + NTILE],
                        wait=(t_rp_dma, lp2_cons["rp"]))
        tload = t_xload
        t_y = []
        for m in range(SC):
            t_hs = V.op("scalar_tensor_tensor", ctile[:, m * NTILE:(m + 1) * NTILE],
                        Ptile[:, m * NTILE:(m + 1) * NTILE], Gpp[:, m:m + 1],
                        ctile[:, m * NTILE:(m + 1) * NTILE], OP.mult, OP.add,
                        wait=(t_lc, t_lP, t_G[m]))
            t_sg = Sc.op("activation", tgtile[:, m * NTILE:(m + 1) * NTILE],
                         tgtile[:, m * NTILE:(m + 1) * NTILE], AF.Sigmoid,
                         bias=bing[:, m:m + 1], scale=1.0, wait=(t_ltg, t_p0_all))
            t_y.append(V.op("tensor_mul", ystage[:, m * NTILE:(m + 1) * NTILE],
                            ctile[:, m * NTILE:(m + 1) * NTILE],
                            tgtile[:, m * NTILE:(m + 1) * NTILE],
                            wait=(t_hs, t_sg, cons_ystage)))
        if debug:
            for m in range(SC):
                t_hs_echo = dECHO.dma(dbg["hs"][m * 128:(m + 1) * 128, n0:n0 + NTILE],
                                    ctile[:, m * NTILE:(m + 1) * NTILE],
                                    wait=(t_y[m], t_hs_echo, t_echo[-1] if t_echo else None))
        lp2_cons["c"] = t_hs_echo if debug else t_y[-1]
        lp2_cons["P"] = t_y[-1]
        lp2_cons["tg"] = t_y[-1]
        t_wo = []
        for m in range(DC):
            tmm = None
            for k in range(SC):
                w0 = (t_y[0], t_y[1], t_p0_all) + ((bank_cons[m],) if k == 0 else ())
                tmm = T.op("matmul", bank(m, NTILE),
                           Wouts[:, k * D + m * 128:k * D + m * 128 + 128],
                           ystage[:, k * NTILE:(k + 1) * NTILE],
                           start=(k == 0), stop=(k == SC - 1), wait=w0, inc=(k == SC - 1))
            t_wo.append(tmm)
        cons_ystage = t_wo[-1]
        t_asm = []
        for c in range(DC):
            dst = xcb[:, c * NTILE:(c + 1) * NTILE]
            t1 = V.op("tensor_sub", dst, xt[:, c * NTILE:(c + 1) * NTILE], m1tile[:, :],
                      wait=(tload, t_lm1) + ((t_x1dma, cons_xcb) if c == 0 else ()))
            t2 = V.op("tensor_mul", dst, dst, rptile[:, :], wait=(t_lrp,))
            t3 = V.op("scalar_tensor_tensor", dst, bank(c, NTILE), bout_pp[:, c:c + 1],
                      dst, OP.add, OP.add, wait=(t_wo[c], t_p0_all))
            bank_cons[c] = t3
            t_asm.append(t3)
        lp2_cons["m1"] = t_asm[-1]
        lp2_cons["rp"] = t_asm[-1]
        if n + 1 < NTILES:
            t_xload = xdma((n + 1) * NTILE, wait=(t_asm[-1],))
        t_x1dma = dX1PP.dma(dx1pp.rearrange("(c p) n -> p c n", p=128)[:, :, n0:n0 + NTILE],
                          xcb[:, :].rearrange("p (c n) -> p c n", c=DC), wait=tuple(t_asm))
    if debug:
        Sc.wait(t_x1dma)
        t_hs_echo = dECHO.dma(dbg["x1pp"], dx1pp, wait=(t_x1dma, t_hs_echo, t_echo[-1]))

    ex_w2 = (t_x1dma, t_asm[-1], t_xload if False else tload, t_cp_dma)
    for ck in range(HC):
        prep2.append(mk_p2(1, dw2T[ck * 128:(ck + 1) * 128, :],
                           w2qT[:, ck * D:(ck + 1) * D],
                           extra=ex_w2 if ck < 2 else ()))
    pump(prep2)
    t_w2q_done = [p2_done[-1], p2_done[-2]]

    Wd = SLABH
    cons_s8 = cons_s8
    cons_tcent = None
    cons_cbuf = t_w2q_done
    cons_outfm = None
    cons_stage = [None, None]
    t_outd = []
    t_stat_lastB = t_stat_last

    def x1dma(col0, wait=()):
        src = dx1pp.rearrange("(c p) n -> p c n", p=128)[:, :, col0:col0 + Wd]
        return dX1S.dma(x1s[:, 0:DC * Wd].rearrange("p (c n) -> p c n", c=DC), src, wait=wait)

    t_x1l = x1dma(0, wait=(t_x1dma, t_asm[-1], cons_ystage))
    for sl in range(NSLABS):
        col0 = SLAB * sl
        x1ch = [x1s[:, c * Wd:(c + 1) * Wd] for c in range(DC)]
        tload = t_x1l
        t_s = G.op("tensor_add", s8[:, 0:Wd], x1ch[0], x1ch[1], wait=(tload, cons_s8))
        for c in range(2, DC):
            t_s = G.op("tensor_add", s8[:, 0:Wd], s8[:, 0:Wd], x1ch[c])
        t_q = None
        t_sqp = [t_stat_lastB, t_stat_lastB]
        for c in range(DC):
            scr = sqscr if c % 2 == 0 else sq2
            t_sq = Sc.op("activation", scr[:, 0:Wd], x1ch[c], AF.Square,
                         wait=(tload, t_sqp[c % 2]))
            t_q = (V.op("tensor_copy", q8[:, 0:Wd], scr[:, 0:Wd], wait=(t_sq, cons_s8))
                   if c == 0 else
                   V.op("tensor_add", q8[:, 0:Wd], q8[:, 0:Wd], scr[:, 0:Wd], wait=(t_sq,)))
            t_sqp[c % 2] = t_q
        t_Sb = T.op("matmul", bank(6, Wd), ones[:, :], s8[:, 0:Wd], start=True, stop=True,
                    wait=(t_s, bank_cons[6]))
        t_Qb = T.op("matmul", bank(7, Wd), ones[:, :], q8[:, 0:Wd], start=True, stop=True,
                    wait=(t_q, bank_cons[7]))
        cons_s8 = t_Qb
        m2b, g1b, s1b, c0, c1, c2 = ch
        t_m2 = V.op("tensor_scalar_mul", m2b[:, 0:Wd], bank(6, Wd), 1.0 / D,
                    wait=(t_Sb, cons_tcent))
        t_qq = V.op("tensor_scalar_mul", c0[:, 0:Wd], bank(7, Wd), 1.0 / D, wait=(t_Qb,))
        bank_cons[6] = t_m2
        bank_cons[7] = t_qq
        t_ms = Sc.op("activation", c1[:, 0:Wd], m2b[:, 0:Wd], AF.Square, wait=(t_m2,))
        t_var = V.op("tensor_sub", c2[:, 0:Wd], c0[:, 0:Wd], c1[:, 0:Wd],
                     wait=(t_ms, t_qq))
        t_vpe = V.op("tensor_scalar_add", c2[:, 0:Wd], c2[:, 0:Wd], 1e-5)
        t_rec = V.op("reciprocal", c0[:, 0:Wd], c2[:, 0:Wd], wait=(t_vpe,))
        t_r2 = Sc.op("activation", c2[:, 0:Wd], c0[:, 0:Wd], AF.Sqrt, wait=(t_rec,))
        t_tc = [V.op("tensor_sub", tcent[:, c * Wd:(c + 1) * Wd], x1ch[c], m2b[:, 0:Wd],
                     wait=(t_m2, tload) + ((cons_tcent,) if c == 0 else ()))
                for c in range(DC)]
        t_red = V.op("tensor_reduce", c0[:, 0:Wd],
                     tcent[:, 0:DC * Wd].rearrange("p (c n) -> p n c", c=DC),
                     AX.X, OP.max, apply_absolute_value=True,
                     wait=tuple(t_tc) + (t_r2,))
        t_par = G.op("partition_all_reduce", c0[:, 0:Wd], c0[:, 0:Wd], 128, RED.max,
                     wait=(t_red,))
        t_mm = V.op("tensor_mul", c1[:, 0:Wd], c0[:, 0:Wd], c2[:, 0:Wd],
                    wait=(t_par, t_r2))
        t_mmc = V.op("tensor_scalar_max", c1[:, 0:Wd], c1[:, 0:Wd], 1e-5)
        t_s1 = V.op("reciprocal", s1b[:, 0:Wd], c1[:, 0:Wd], wait=(t_mmc,))
        t_s1 = V.op("tensor_scalar_mul", s1b[:, 0:Wd], s1b[:, 0:Wd], 127.0)
        t_iv = V.op("tensor_scalar_mul", g1b[:, 0:Wd], c1[:, 0:Wd], 1.0 / 127.0)
        t_iv = V.op("tensor_scalar_mul", g1b[:, 0:Wd], g1b[:, 0:Wd], mcw[:, 0:1],
                    wait=(t_mc[0],))
        t_xq = []
        for c in range(DC):
            scr = sqscr if c % 2 == 0 else sq2
            tq0 = V.op("tensor_mul", scr[:, 0:Wd], tcent[:, c * Wd:(c + 1) * Wd],
                       c2[:, 0:Wd], wait=(t_tc[c], t_r2, t_red))
            tq1 = V.op("tensor_mul", scr[:, 0:Wd], scr[:, 0:Wd], s1b[:, 0:Wd],
                       wait=(t_s1,))
            tq2 = Sc.op("activation", scr[:, 0:Wd], scr[:, 0:Wd], AF.Copy,
                        bias=MAGIC, scale=1.0, wait=(tq1,))
            t_xq.append(V.op("tensor_scalar_sub", xq1[:, c * Wd:(c + 1) * Wd],
                             scr[:, 0:Wd], MAGIC, wait=(tq2,)))
        t_stat_lastB = t_xq[-1]
        if debug and sl == 0:
            t_echo = [dECHO.dma(dbg["xq1e"], xq1[:, :], wait=(t_xq[-1], t_echo[-1]))]
            t_echo += [dECHO.dma(dbg["sc2e"][:, 0:NTILE], m2b[:, :], wait=(t_m2, t_echo[-1]))]
            t_echo += [dECHO.dma(dbg["sc2e"][:, NTILE:2 * NTILE], c2[:, :], wait=(t_r2, t_echo[-1]))]
            t_echo += [dECHO.dma(dbg["sc2e"][:, 2 * NTILE:3 * NTILE], s1b[:, :], wait=(t_s1, t_echo[-1]))]
            t_echo += [dECHO.dma(dbg["sc2e"][:, 3 * NTILE:4 * NTILE], g1b[:, :], wait=(t_iv, t_echo[-1]))]
        t_h2 = []
        t_w1mm_last = None
        for f in range(HC):
            bk = f % 4
            tmm = None
            for k in range(DC):
                w0 = [t_xq[k]]
                if k == 0:
                    w0 += [t_w1q_done[0], t_w1q_done[1], bank_cons[bk]]
                tmm = T.op("matmul", bank(bk, Wd),
                           w1qT[:, k * H + f * 128:k * H + f * 128 + 128],
                           xq1[:, k * Wd:(k + 1) * Wd],
                           start=(k == 0), stop=(k == DC - 1), wait=tuple(w0),
                           inc=(k == DC - 1))
            t_w1mm_last = tmm
            hb = hscr[:, (f % 2) * Wd:(f % 2) * Wd + Wd]
            t_h = V.op("tensor_mul", hb, bank(bk, Wd), g1b[:, 0:Wd], wait=(tmm, t_iv))
            bank_cons[bk] = t_h
            if sl == 0:
                t_h = V.op("tensor_scalar_mul", hb[:, 0:1], hb[:, 0:1], masks[:, 0:1],
                           wait=(t_p0_all,))
            if sl == NSLABS - 1:
                t_h = V.op("tensor_scalar_mul", hb[:, Wd - 1:Wd], hb[:, Wd - 1:Wd],
                           masks[:, 1:2], wait=(t_p0_all,))
            if debug and sl == 0 and f < 2:
                t_echo = [dECHO.dma(dbg["he"][:, f * SLABH:(f + 1) * SLABH], hb,
                                    wait=(t_h, t_echo[-1]))]
                t_h = t_echo[-1]
            csl = cbuf[:, f * SLAB:(f + 1) * SLAB]
            cw0 = (t_h, t_p0_all, t_p0_all)
            if f == 0:
                cw0 += tuple(cons_cbuf) if isinstance(cons_cbuf, list) else (cons_cbuf,)
            t_c1 = Sc.op("activation", csl, hb[:, 1:1 + SLAB],
                         AF.Identity, bias=cb[:, f:f + 1], scale=cw[:, 3 * f + 1:3 * f + 2],
                         wait=cw0)
            t_c2 = V.op("scalar_tensor_tensor", csl, hb[:, 0:SLAB],
                        cw[:, 3 * f:3 * f + 1], csl, OP.mult, OP.add, wait=(t_c1,))
            t_c3 = V.op("scalar_tensor_tensor", csl, hb[:, 2:2 + SLAB],
                        cw[:, 3 * f + 2:3 * f + 3], csl, OP.mult, OP.add)
            t_h2.append(Sc.op("activation", csl, csl, AF.Silu, wait=(t_c3,)))
        cons_tcent = t_w1mm_last
        t_red2 = V.op("tensor_reduce", c0[:, 0:SLAB],
                      cbuf[:, 0:HC * SLAB].rearrange("p (c n) -> p n c", c=HC),
                      AX.X, OP.max, apply_absolute_value=True, wait=tuple(t_h2))
        t_par2 = G.op("partition_all_reduce", c0[:, 0:SLAB], c0[:, 0:SLAB], 128, RED.max,
                      wait=(t_red2,))
        t_mmc2 = V.op("tensor_scalar_max", c1[:, 0:SLAB], c0[:, 0:SLAB], 1e-5,
                      wait=(t_par2,))
        t_s2 = V.op("reciprocal", c2[:, 0:SLAB], c1[:, 0:SLAB], wait=(t_mmc2,))
        t_s2 = V.op("tensor_scalar_mul", c2[:, 0:SLAB], c2[:, 0:SLAB], 127.0)
        t_iv2 = V.op("tensor_scalar_mul", s1b[:, 0:SLAB], c1[:, 0:SLAB], 1.0 / 127.0)
        t_iv2 = V.op("tensor_scalar_mul", s1b[:, 0:SLAB], s1b[:, 0:SLAB], mcw[:, 1:2],
                     wait=(t_mc[1],))
        t_ma = V.op("tensor_mul", c1[0:1, 0:SLAB], m1row[0:1, col0 + 1:col0 + 1 + SLAB],
                    c2[0:1, 0:SLAB], wait=(t_s2, t_m1row))
        t_ma = V.op("tensor_scalar_mul", c1[0:1, 0:SLAB], c1[0:1, 0:SLAB], rw2[0:1, 0:1],
                    wait=(t_rw2,))
        t_ma = V.op("tensor_copy", madj[0:1, 0:SLAB], c1[0:1, 0:SLAB])
        t_h2echo = None
        if debug and sl == 0:
            t_h2echo = dECHO.dma(dbg["h2e"], cbuf[:, :], wait=tuple(t_h2) + (t_echo[-1],))
            t_echo = [t_h2echo]
            t_echo += [dECHO.dma(dbg["sc2e"][:, 4 * NTILE:5 * NTILE], c2[:, :],
                                 wait=(t_s2, t_echo[-1]))]
            t_echo += [dECHO.dma(dbg["sc2e"][:, 5 * NTILE:6 * NTILE], s1b[:, :],
                                 wait=(t_iv2, t_echo[-1]))]
        t_x2 = []
        for f in range(HC):
            scr = sqscr if f % 2 == 0 else sq2
            csl = cbuf[:, f * SLAB:(f + 1) * SLAB]
            tq0 = V.op("tensor_mul", scr[:, 0:SLAB], csl, c2[:, 0:SLAB],
                       wait=(t_h2[f], t_s2, t_red2))
            tq1 = Sc.op("activation", scr[:, 0:SLAB], scr[:, 0:SLAB], AF.Copy,
                        bias=MAGIC, scale=1.0, wait=(tq0,))
            t_x2.append(V.op("tensor_scalar_sub", xq2[:, f * SLAB:(f + 1) * SLAB],
                             scr[:, 0:SLAB], MAGIC,
                             wait=(tq1,) + ((t_h2echo,) if (debug and sl == 0 and f == 0) else ())))
        t_stat_lastB = t_x2[-1]
        if debug and sl == 0:
            t_echo = [dECHO.dma(dbg["xq2e"], xq2[:, :], wait=(t_x2[-1], t_echo[-1]))]
        t_w2mm_last = None
        for m in range(DC):
            bk = 4 + (m % 2)
            tmm = None
            for k in range(HC):
                w0 = [t_x2[k]]
                if k == 0:
                    w0 += [t_w2q_done[0], t_w2q_done[1], bank_cons[bk]]
                tmm = T.op("matmul", bank(bk, SLAB),
                           w2qT[:, k * D + m * 128:k * D + m * 128 + 128],
                           xq2[:, k * SLAB:(k + 1) * SLAB],
                           start=(k == 0), stop=False, wait=tuple(w0), inc=False)
            tmm = T.op("matmul", bank(bk, SLAB), onesbf[0:1, :], madj[0:1, 0:SLAB],
                       start=False, stop=True, wait=(t_ma, t_onesbf))
            t_w2mm_last = tmm
            t_e1 = V.op("tensor_mul", outfm[:, :], bank(bk, SLAB), s1b[:, 0:SLAB],
                        wait=(tmm, t_iv2, cons_outfm))
            bank_cons[bk] = t_e1
            t_e2 = V.op("tensor_add", outfm[:, :], outfm[:, :],
                        x1s[:, m * Wd + 1:m * Wd + 1 + SLAB])
            if debug and sl == 0:
                t_echo = [dECHO.dma(dbg["outfme"][:, m * SLAB:(m + 1) * SLAB], outfm[:, :],
                                    wait=(t_e2, t_echo[-1]))]
                t_e2 = t_echo[-1]
            for tb in range(2):
                tbk = 6 + tb
                t_tp = T.op("transpose", bank(tbk, 128), outfm[:, tb * 128:(tb + 1) * 128],
                            ident[:, :], wait=(t_e2, bank_cons[tbk]))
                cons_outfm = t_tp
                t_ev = Sc.op("copy", out=stage128[:, tb * 128:(tb + 1) * 128],
                             in_=bank(tbk, 128), wait=(t_tp, cons_stage[tb]))
                bank_cons[tbk] = t_ev
                t_od = (dOUT0 if tb == 0 else dOUT1).dma(
                    dout[sl * SLAB + tb * 128: sl * SLAB + (tb + 1) * 128,
                         m * 128:(m + 1) * 128],
                    stage128[:, tb * 128:(tb + 1) * 128], wait=(t_ev,))
                cons_stage[tb] = t_od
                t_outd.append(t_od)
        cons_cbuf = t_w2mm_last
        if sl + 1 < NSLABS:
            t_x1l = x1dma(SLAB * (sl + 1), wait=(t_xq[-1], t_s, t_q, t_tc[-1], t_e2))

    if debug:
        Sc.wait(t_outd[-1])
        te_ = dECHO.dma(dbg["sm"][:, 0:2], mcw[:, :], wait=(t_mc[0], t_mc[1], t_hs_echo))
        te_ = dECHO.dma(dbg["sm"][:, 2:4], swpp[:, :], wait=(t_sw, te_))
        te_ = dECHO.dma(dbg["sm"][:, 4:6], Gpp[:, :], wait=tuple(t_G) + (te_,))
        te_ = dECHO.dma(dbg["sm"][:, 6:10], stg4[:, :], wait=(t_sum_tok, te_))
        te_ = dECHO.dma(dbg["sm"][:, 10:42], gath[:, :], wait=(t_ccr, te_))
        te_ = dECHO.dma(dbg["sm"][:, 42:44], wacc[:, :], wait=(p1_add[0], te_))
        te_ = dECHO.dma(dbg["m1"], dm1sp, wait=(t_m1_dma, te_))
        te_ = dECHO.dma(dbg["tg"], dtgsp, wait=(t_tg_dma, te_))
        Sc.raw.wait_ge(dECHO.sem, dECHO.n)
    Sc.wait(t_outd[-1], t_x1dma)
    nc.finalize()
    return nc


def _host_prep(inputs):
    x = np.asarray(inputs["L0_attn_x"])
    x_fm = np.ascontiguousarray(np.asarray(x[0]).T)
    w1T = np.ascontiguousarray(np.asarray(inputs["L0_ffn_w1"]).T)
    w2T = np.ascontiguousarray(np.asarray(inputs["L0_ffn_w2"]).T)
    ident = np.eye(128, dtype=np.float32)

    def ppvec(v, nchunk):
        return np.ascontiguousarray(np.asarray(v, np.float32).reshape(nchunk, 128).T)

    def ppmat(w, kc, n):
        return np.ascontiguousarray(
            np.asarray(w, np.float32).reshape(kc, 128, n).transpose(1, 0, 2).reshape(128, kc * n))

    b_in = np.asarray(inputs["L0_attn_b_in"], np.float32)
    pp = {
        "W_in": np.ascontiguousarray(np.asarray(inputs["L0_attn_W_in"], np.float32)),
        "w_norm": ppvec(inputs["L0_attn_w_norm"], 8),
        "b_inu": ppvec(b_in[0:512], 4),
        "b_ing": ppvec(b_in[512:768], 2),
        "W_x": ppmat(inputs["L0_attn_W_x"], 2, S),
        "b_x": np.ascontiguousarray(np.asarray(inputs["L0_attn_b_x"], np.float32)),
        "W_dt": ppmat(inputs["L0_attn_W_dt"], 2, S),
        "b_dt": np.ascontiguousarray(np.asarray(inputs["L0_attn_b_dt"], np.float32)),
        "W_out": ppmat(inputs["L0_attn_W_out"], 2, D),
        "b_out": ppvec(inputs["L0_attn_b_out"], 8),
        "conv_w": ppmat(np.asarray(inputs["L0_ffn_conv_w"], np.float32).reshape(H, 3), 32, 3),
        "conv_b": ppvec(inputs["L0_ffn_conv_b"], 32),
    }
    per_core = []
    for i in range(8):
        xs = np.zeros((D, NT), np.float32)
        g0 = 2048 * i - 1
        lo, hi = max(g0, 0), min(g0 + NT, 16384)
        xs[:, lo - g0: hi - g0] = x_fm[:, lo:hi]
        onehot = np.zeros((128, 8), np.float32)
        if i > 0:
            onehot[:, i - 1] = 1.0
        msk = np.ones((128, 2), np.float32)
        if i == 0:
            msk[:, 0] = 0.0
        if i == 7:
            msk[:, 1] = 0.0
        per_core.append({
            "x": xs,
            "W_in": pp["W_in"], "w_norm": pp["w_norm"],
            "b_inu": pp["b_inu"], "b_ing": pp["b_ing"],
            "W_x": pp["W_x"], "b_x": pp["b_x"], "W_dt": pp["W_dt"], "b_dt": pp["b_dt"],
            "W_out": pp["W_out"], "b_out": pp["b_out"],
            "w1T": w1T, "w2T": w2T,
            "conv_w": pp["conv_w"], "conv_b": pp["conv_b"],
            "ident": ident, "onehot": onehot, "masks": msk,
        })
    return per_core


_CACHE = {}


def kernel(**inputs):
    if "prog" not in _CACHE:
        _CACHE["prog"] = build_program()
    nc = _CACHE["prog"]
    in_maps = _host_prep(inputs)
    res = run_bass_kernel_spmd(nc, in_maps, core_ids=list(range(8)))
    out = np.concatenate([np.asarray(res.results[i]["y"]) for i in range(8)], axis=0)
    return out.reshape(1, 16384, D)
